# version 06 (final)
# baseline (speedup 1.0000x reference)
import os
import numpy as np
import ml_dtypes
import concourse.bass as bass
import concourse.mybir as mybir
from concourse.bass_utils import run_bass_kernel_spmd

os.environ.setdefault("BASS_INTERP_NUM_WORKERS", "8")

F32 = mybir.dt.float32
BF16 = mybir.dt.bfloat16
AF = mybir.ActivationFunctionType
ALU = mybir.AluOpType
AX = mybir.AxisListType
bfd = ml_dtypes.bfloat16

H, E, NH, PD, NST, KC, Q, FF = 1024, 2048, 32, 64, 128, 4, 128, 4096
CONV = E + 2 * NST
DIN = 2 * E + 2 * NST + NH
Bb, T, NCORE, W = 8, 4096, 8, 512
WROW = 640
EPS, NEPS, LNEPS = 1e-6, 1e-5, 1e-5
HP = NH * PD


class Sem:
    __slots__ = ("name", "n", "h")

    def __init__(self, name):
        self.name, self.n, self.h = name, 0, None


class Prog:
    ENGINES = ("sync", "scalar", "vector", "tensor", "gpsimd")

    def __init__(self):
        self.ops = {e: [] for e in self.ENGINES}
        self.sems = []
        self.lim = int(os.environ.get("KTRUNC", "99"))
        self.cur = 0
        self.selfw = {}
        self.done = {}

    def phase(self, k):
        self.cur = k

    def on(self):
        return self.cur <= self.lim

    def sem(self, name):
        s = Sem(name)
        self.sems.append(s)
        return s

    def raw(self, engine, fn):
        self.ops[engine].append(fn)

    def wait(self, engine, sem, val=None):
        if not self.on():
            return
        v = sem.n if val is None else val
        if v <= 0:
            return
        self.raw(engine, lambda eng, s=sem, vv=v: eng.wait_ge(s.h, vv))

    def op(self, engine, fn, sem=None, inc=1):
        if not self.on():
            return
        if engine in ("vector", "scalar"):
            last = self.selfw.get(engine)
            if last is not None and last[1] > self.done.get(engine, 0):
                self.raw(engine, lambda eng, s=last[0], v=last[1]: eng.wait_ge(s.h, v))
                self.done[engine] = last[1]
        if sem is None:
            self.raw(engine, fn)
        else:
            sem.n += inc
            self.raw(engine, lambda eng, f=fn, s=sem, i=inc: f(eng).then_inc(s.h, i))
            if engine in ("vector", "scalar"):
                self.selfw[engine] = (sem, sem.n)

    def dma(self, engine, out, in_, sem, **kw):
        if not self.on():
            return
        sem.n += 16
        self.raw(engine, lambda eng, o=out, i=in_, s=sem, k=kw:
                 eng.dma_start(out=o, in_=i, **k).then_inc(s.h, 16))

    def dmat(self, engine, out, in_, sem):
        if not self.on():
            return
        sem.n += 16
        self.raw(engine, lambda eng, o=out, i=in_, s=sem:
                 eng.dma_start_transpose(o, i).then_inc(s.h, 16))


class SB:

    def __init__(self, nc, budget=None):
        self.nc = nc
        arena = (nc.sbuf_bytes_remaining - 4096) & ~31
        r = nc.bump_sbuf(arena)
        assert r is not None
        self.base = r[0]
        self.budget = arena if budget is None else budget
        self.off, self.phase_base, self.peak = 0, None, 0
        self.uid = 0

    def _alloc(self, name, shape, dtype):
        es = mybir.dt.size(dtype)
        nbytes = es * int(np.prod(shape[1:]))
        off = (self.off + 31) & ~31
        self.uid += 1
        t = self.nc.alloc_sbuf_tensor_at(f"{name}_{self.uid}", list(shape), dtype,
                                         offset=self.base + off)
        self.off = off + nbytes
        self.peak = max(self.peak, self.off)
        assert self.off <= self.budget, (name, self.off)
        return t

    def res(self, name, shape, dtype):
        assert self.phase_base is None
        return self._alloc(name, shape, dtype)

    def open_phase(self):
        if self.phase_base is None:
            self.phase_base = self.off
        self.off = self.phase_base

    def ph(self, name, shape, dtype):
        assert self.phase_base is not None
        return self._alloc(name, shape, dtype)


def _stage_inputs(inp):
    x = np.ascontiguousarray(inp["L0_attn_hidden_states"])
    f32 = np.float32

    def t2(w):
        wt = np.ascontiguousarray(np.transpose(w))
        k = wt.shape[0] // 128
        return np.ascontiguousarray(wt.reshape(k, 128, wt.shape[1]))

    def col(v, k):
        return np.ascontiguousarray(v.reshape(k, 128).T).astype(f32)

    wg = np.concatenate([inp["L0_ffn_wf"], inp["L0_ffn_wi"], inp["L0_ffn_wc"]], 0)
    ug = np.concatenate([inp["L0_ffn_uf"], inp["L0_ffn_ui"], inp["L0_ffn_uc"]], 0)
    shared = dict(
        w_inT=t2(inp["L0_attn_w_in"]), w_outT=t2(inp["L0_attn_w_out"]),
        w_gateT=t2(inp["L0_attn_w_gate"]), w_upT=t2(inp["L0_attn_w_up"]),
        w_downT=t2(inp["L0_attn_w_down"]), wgT=t2(wg), ugT=t2(ug),
        woT=t2(inp["L0_ffn_wo"]), uoT=t2(inp["L0_ffn_uo"]),
        gam1=col(inp["L0_attn_w_ln1"], 8), gam2=col(inp["L0_attn_w_ln2"], 8),
        gammn=col(inp["L0_attn_w_mnorm"], 16), bconv=col(inp["L0_attn_b_conv"], 18),
        wconv=np.ascontiguousarray(
            inp["L0_attn_w_conv"].reshape(18, 128, KC).transpose(1, 0, 2)).astype(f32),
        bg=col(np.concatenate([inp["L0_ffn_bf"], inp["L0_ffn_bi"], inp["L0_ffn_bc"]]), 24),
        bo=col(inp["L0_ffn_bo"], 8), inc0=col(inp["L0_ffn_in_c"], 8),
        dtb=np.ascontiguousarray(inp["L0_attn_dt_bias"].reshape(NH, 1)).astype(f32),
        alog=np.ascontiguousarray(inp["L0_attn_A_log"].reshape(NH, 1)).astype(f32),
        dvec=np.ascontiguousarray(inp["L0_attn_D"].reshape(1, NH)).astype(f32),
    )
    in_maps = []
    for j in range(NCORE):
        lo = j * W
        xs = np.zeros((Bb, WROW, H), f32)
        s0 = lo - 3
        src = x[:, max(s0, 0):min(lo + W + 125, T), :]
        d0 = max(0, -s0)
        xs[:, d0:d0 + src.shape[1], :] = src
        mixcol = np.zeros((32, 64), f32)
        maskinit = np.zeros((32, 64), f32)
        for i in range(8):
            for k in range(8):
                if i < k < j:
                    mixcol[:, 8 * i + k] = 1.0
            if i >= j:
                maskinit[:, 8 * i:8 * (i + 1)] = -1e9
        m = dict(shared)
        m["xin"] = xs
        m["mixcol"] = mixcol
        m["maskinit"] = maskinit
        in_maps.append(m)
    return in_maps


def build():
    nc = bass.Bass("TRN2", target_bir_lowering=False, detect_race_conditions=False)
    p = Prog()

    d = {}
    for name, shape in [
        ("xin", [Bb, WROW, H]), ("w_inT", [8, 128, DIN]), ("w_outT", [16, 128, H]),
        ("w_gateT", [8, 128, FF]), ("w_upT", [8, 128, FF]), ("w_downT", [32, 128, H]),
        ("wgT", [8, 128, 3 * H]), ("ugT", [8, 128, 3 * H]), ("woT", [8, 128, H]),
        ("uoT", [8, 128, H]), ("gam1", [128, 8]), ("gam2", [128, 8]),
        ("gammn", [128, 16]), ("bconv", [128, 18]), ("wconv", [128, 18, KC]),
        ("bg", [128, 24]), ("bo", [128, 8]), ("inc0", [128, 8]),
        ("dtb", [NH, 1]), ("alog", [NH, 1]), ("dvec", [1, NH]),
        ("mixcol", [32, 64]), ("maskinit", [32, 64]),
    ]:
        d[name] = nc.dram_tensor(name, shape, F32, kind="ExternalInput").ap()
    out_d = nc.dram_tensor("out", [Bb, W, H], F32, kind="ExternalOutput").ap()

    _dbg = os.environ.get("KDBG", "") == "1"

    def idram(name, shape, dtype):
        k = "ExternalOutput" if (_dbg and not name.startswith("g")) else "Internal"
        return nc.dram_tensor(name, shape, dtype, kind=k)

    h1sp = idram("h1sp", [Bb, WROW, H], BF16)
    xnsp = idram("xnsp", [Bb, WROW, H], BF16)
    statsp = idram("statsp", [Bb, WROW, 4], F32)
    xpsp = idram("xpsp", [Bb, 128, 16, W], BF16)
    bsp = idram("bsp", [Bb, 128, W], BF16)
    csp = idram("csp", [Bb, 128, W], BF16)
    szsp = idram("szsp", [Bb, 128, 16, W], BF16)
    dtsp = idram("dtsp", [Bb, NH, 515], F32)
    ysp = idram("ysp", [Bb, 4, 128, HP], BF16)
    hsp = idram("hsp", [Bb, 4, 128, H], BF16)
    hidsp = idram("hidsp", [Bb, 4, 128, H], F32)
    h2sp = idram("h2sp", [Bb, 4, 128, H], BF16)
    mlpacc = idram("mlpacc", [Bb, 4, 128, H], F32)
    dbgsin = idram("dbgsin", [Bb, 128, HP], BF16)
    dbgw = idram("dbgw", [3, NH, 64], F32)
    dbgg = idram("dbgg", [Bb, 4, 128, HP], BF16)
    dbgn = idram("dbgn", [Bb, 4, 128, 8], F32)
    gin = idram("gin", [Bb, 128, HP], BF16)
    gout = idram("gout", [NCORE, Bb, 128, HP], BF16)
    gin2 = idram("gin2", [NH, 8], F32)
    gout2 = idram("gout2", [NCORE, NH, 8], F32)

    i_trilT = nc.inline_tensor(np.triu(np.ones((128, 128), bfd)), name="c_trilT")
    i_triu = nc.inline_tensor(np.triu(np.ones((128, 128), bfd)), name="c_triu")
    i_negleq = nc.inline_tensor(
        np.ascontiguousarray(-np.tril(np.ones((128, 128))).T).astype(bfd), name="c_negleq")
    i_ones1 = nc.inline_tensor(np.ones((1, 128), bfd), name="c_ones1")
    i_all1 = nc.inline_tensor(np.ones((128, 128), bfd), name="c_all1")
    i_ones32f = nc.inline_tensor(np.ones((32, 128), np.float32), name="c_ones32f")
    i_o32x128 = nc.inline_tensor(np.ones((32, 128), np.float32), name="c_o32x128")
    i_eye32 = nc.inline_tensor(np.eye(32, dtype=np.float32), name="c_eye32")
    i_eye32b = nc.inline_tensor(np.eye(32, dtype=bfd), name="c_eye32b")
    i_o32b = nc.inline_tensor(np.ones((32, 128), bfd), name="c_o32b")
    i_zero = nc.inline_tensor(np.zeros((128, 512), np.float32), name="c_zero")
    CVALS = [LNEPS, EPS, -0.5, -1.0, 1.0 / HP, 1.0 / H, 1.0]
    i_cv = nc.inline_tensor(
        np.broadcast_to(np.asarray(CVALS, np.float32), (128, len(CVALS))).copy(),
        name="c_cv")

    sL = p.sem("sL")
    sS = p.sem("sS")
    sG = p.sem("sG")
    sGC = p.sem("sGC")
    sv = p.sem("sv")
    st = p.sem("st")
    ss = p.sem("ss")

    def fence():
        counts = [(s, s.n) for s in (sv, st, ss, sL, sS, sG)]
        for e in Prog.ENGINES:
            for s_, n_ in counts:
                p.wait(e, s_, n_)

    sb = SB(nc)
    c_trilT = sb.res("trilT", [128, 128], BF16)
    c_triu = sb.res("triu", [128, 128], BF16)
    c_negleq = sb.res("negleq", [128, 128], BF16)
    c_ones1 = sb.res("ones1", [1, 128], BF16)
    c_all1 = sb.res("all1", [128, 128], BF16)
    c_ones32f = sb.res("ones32f", [32, 128], F32)
    c_o32x128 = sb.res("o32x128", [32, 128], F32)
    c_eye32 = sb.res("eye32", [32, 32], F32)
    c_eye32b = sb.res("eye32b", [32, 32], BF16)
    c_o32x128b = sb.res("o32x128b", [32, 128], BF16)
    c_zero = sb.res("zero", [128, 512], F32)
    c_cv = sb.res("cv", [128, 7], F32)
    for _i, _v in enumerate(CVALS):
        nc.const_aps.register(F32, float(_v), c_cv[:, _i:_i + 1])
    gam1_s = sb.res("gam1", [128, 8], F32)
    gam2_s = sb.res("gam2", [128, 8], F32)
    gammn_s = sb.res("gammn", [128, 16], F32)
    bconv_s = sb.res("bconv", [128, 18], F32)
    wconv_s = sb.res("wconv", [128, 18 * KC], F32)
    bg_s = sb.res("bg", [128, 24], F32)
    bo_s = sb.res("bo", [128, 8], F32)
    inc0_s = sb.res("inc0", [128, 8], F32)
    dtb_s = sb.res("dtb", [NH, 1], F32)
    alog_s = sb.res("alog", [NH, 1], F32)
    acol_s = sb.res("acol", [NH, 1], F32)
    dvec_s = sb.res("dvec", [1, NH], F32)
    dvecb_s = sb.res("dvecb", [1, NH], BF16)
    dfull_s = sb.res("dfull", [128, HP], BF16)
    expcsD = sb.res("expcsD", [128, 32 * NH], F32)
    wt_s = sb.res("wt", [NH, 64], F32)
    dtot_s = sb.res("dtot", [NH, 8], F32)

    PS = [nc.alloc_psum_tensor(f"ps{i}", [128, 512], F32) if i != 1 else
          nc.alloc_psum_tensor("ps1b", [128, 1024], BF16) for i in range(8)]
    psum_guard = {}

    def bank_wait(engine, banks):
        for bk in banks:
            for s_, n_ in psum_guard.get(bk, {}).values():
                p.wait(engine, s_, n_)

    def mark_bank(bk, sem):
        psum_guard.setdefault(bk, {})[sem.name] = (sem, sem.n)

    def TT(engine, out, a, b_, op=ALU.mult, sem=None):
        p.op(engine, lambda e, o=out, x=a, y=b_, oo=op: e.tensor_tensor(o, x, y, oo), sem)

    def rep_h(colap, reps=PD, nh=NH):
        return colap.unsqueeze(2).broadcast_to((128, nh, reps))

    sb.open_phase()
    xb = [sb.ph("p0_x", [128, H], F32) for _ in range(2)]
    sqd = sb.ph("p0_sqd", [128, H], F32)
    stc = sb.ph("p0_stc", [128, 16], F32)
    statw = sb.ph("p0_statw", [128, 4], F32)
    h1b = sb.ph("p0_h1b", [128, H], BF16)
    xnb2 = sb.ph("p0_xnb", [128, H], BF16)

    for dst, src in [(c_trilT, i_trilT), (c_triu, i_triu), (c_negleq, i_negleq),
                     (c_ones1, i_ones1), (c_all1, i_all1), (c_ones32f, i_ones32f),
                     (c_o32x128, i_o32x128), (c_eye32, i_eye32),
                     (c_eye32b, i_eye32b), (c_o32x128b, i_o32b), (c_zero, i_zero),
                     (c_cv, i_cv)]:
        p.dma("sync", dst[:], src.ap(), sL)
    for dst, src in [(gam1_s, d["gam1"]), (gam2_s, d["gam2"]), (gammn_s, d["gammn"]),
                     (bconv_s, d["bconv"]), (bg_s, d["bg"]), (bo_s, d["bo"]),
                     (inc0_s, d["inc0"]), (dtb_s, d["dtb"]), (alog_s, d["alog"]),
                     (dvec_s, d["dvec"])]:
        p.dma("sync", dst[:], src, sL)
    p.dma("sync", wconv_s[:].rearrange("p (a b) -> p a b", a=18), d["wconv"], sL)
    ld0 = sL.n

    p.wait("scalar", sL, ld0)
    p.op("scalar", lambda e: e.activation(acol_s[:], alog_s[:], AF.Exp), ss)
    p.wait("vector", ss, ss.n)
    p.op("vector", lambda e: e.tensor_scalar_mul(acol_s[:], acol_s[:], -1.0), sv)
    p.wait("vector", sL, ld0)
    p.op("vector", lambda e: e.tensor_copy(dvecb_s[:], dvec_s[:]), sv)
    p.wait("tensor", sv, sv.n)
    for i in range(4):
        rhsv = dvecb_s[0:1, 8 * i:8 * (i + 1)].unsqueeze(2).broadcast_to((1, 8, PD))
        p.op("tensor", lambda e, i=i, r=rhsv: e.matmul(
            PS[4 + i][:, 0:512], c_ones1[:], r, start=True, stop=True), st)
    p.wait("vector", st, st.n)
    for i in range(4):
        p.op("vector", lambda e, i=i: e.tensor_copy(
            dfull_s[:, 512 * i:512 * (i + 1)], PS[4 + i][:, 0:512]), sv)
        mark_bank(4 + i, sv)

    for b in range(Bb):
        for tl in range(5):
            xt_ = xb[tl % 2]
            p.wait("sync", sv, sv.n)
            p.dma("sync", xt_[:], d["xin"][b, 128 * tl:128 * (tl + 1), :], sL)
            p.wait("vector", sL, sL.n)
            p.op("vector", lambda e, x=xt_: e.reduce_sum(stc[:, 0:1], x[:], AX.X), sv)
            p.wait("scalar", sL, sL.n)
            p.op("scalar", lambda e, x=xt_: e.activation(
                sqd[:], x[:], AF.Square, accum_out=stc[:, 1:2]), ss)
            p.op("vector", lambda e: e.tensor_scalar_mul(stc[:, 0:1], stc[:, 0:1], 1.0 / H), sv)
            p.wait("vector", ss, ss.n)
            p.op("vector", lambda e: e.tensor_scalar_mul(stc[:, 1:2], stc[:, 1:2], 1.0 / H), sv)
            p.op("vector", lambda e: e.tensor_scalar(
                stc[:, 2:3], stc[:, 0:1], stc[:, 0:1], None, ALU.mult), sv)
            TT("vector", stc[:, 3:4], stc[:, 1:2], stc[:, 2:3], ALU.subtract, sv)
            p.wait("scalar", sv, sv.n)
            p.op("scalar", lambda e: e.activation(stc[:, 4:5], stc[:, 3:4], AF.Ln, bias=LNEPS), ss)
            p.op("scalar", lambda e: e.activation(stc[:, 5:6], stc[:, 4:5], AF.Exp, scale=-0.5), ss)
            p.wait("vector", ss, ss.n)
            p.op("vector", lambda e: e.tensor_scalar(
                stc[:, 6:7], stc[:, 3:4], stc[:, 5:6], stc[:, 5:6], ALU.mult, ALU.mult), sv)
            p.wait("scalar", sv, sv.n)
            p.op("scalar", lambda e: e.activation(stc[:, 7:8], stc[:, 6:7], AF.Ln, bias=EPS), ss)
            p.op("scalar", lambda e: e.activation(stc[:, 8:9], stc[:, 7:8], AF.Exp, scale=-0.5), ss)
            p.wait("vector", ss, ss.n)
            p.op("vector", lambda e: e.tensor_copy(statw[:, 0:1], stc[:, 5:6]), sv)
            p.op("vector", lambda e: e.tensor_scalar(
                statw[:, 1:2], stc[:, 0:1], stc[:, 5:6], None, ALU.mult), sv)
            p.op("vector", lambda e: e.tensor_scalar(
                statw[:, 2:3], stc[:, 5:6], stc[:, 8:9], None, ALU.mult), sv)
            p.op("vector", lambda e: e.tensor_scalar(
                statw[:, 3:4], stc[:, 0:1], statw[:, 2:3], None, ALU.mult), sv)
            p.op("vector", lambda e, x=xt_: e.tensor_scalar(
                xnb2[:], x[:], statw[:, 0:1], statw[:, 1:2], ALU.mult, ALU.subtract), sv)
            p.op("vector", lambda e, x=xt_: e.tensor_scalar(
                h1b[:], x[:], statw[:, 2:3], statw[:, 3:4], ALU.mult, ALU.subtract), sv)
            p.wait("scalar", sv, sv.n)
            p.dma("scalar", xnsp.ap()[b, 128 * tl:128 * (tl + 1), :], xnb2[:], sS)
            p.dma("scalar", h1sp.ap()[b, 128 * tl:128 * (tl + 1), :], h1b[:], sS)
            p.dma("scalar", statsp.ap()[b, 128 * tl:128 * (tl + 1), :], statw[:], sS)
            p.wait("vector", sS, sS.n)

    sb.open_phase()
    winT = sb.ph("winT", [128, 8 * DIN], BF16)
    h1fm = sb.ph("h1fm", [128, 8 * WROW], BF16)
    z_fm = sb.ph("z_fm", [128, 16 * 515], BF16)
    xbc_fm = sb.ph("xbc_fm", [128, 18 * 515], BF16)
    cacc = sb.ph("cacc", [128, W], BF16)
    csil = sb.ph("csil", [128, W], BF16)
    caccB = sb.ph("caccB", [128, W], BF16)
    csilB = sb.ph("csilB", [128, W], BF16)
    csilzA = sb.ph("csilzA", [128, W], BF16)
    csilzB = sb.ph("csilzB", [128, W], BF16)
    dtraw = sb.ph("dtraw", [32, 515], F32)

    p.phase(2)
    fence()
    p.dma("gpsimd", winT[:].rearrange("p (k d) -> p k d", k=8), d["w_inT"]
          .rearrange("k p d -> p k d"), sG)
    wk_done = sG.n

    for b in range(Bb):
        p.wait("sync", sv, sv.n)
        p.wait("sync", st, st.n)
        p.wait("scalar", sS, sS.n)
        p.wait("vector", ss, ss.n)
        p.dmat("sync", h1fm[:].rearrange("p (k t) -> p k t", k=8), h1sp.ap()[b], sL)
        h1_ld = sL.n
        p.wait("vector", sL, h1_ld)
        for k in range(8):
            sl = h1fm[:, k * WROW:(k + 1) * WROW]
            p.op("vector", lambda e, s=sl, k=k: e.tensor_scalar_mul(
                s, s, gam1_s[:, k:k + 1]), sv)
        gready = sv.n
        p.wait("tensor", sv, gready)
        p.wait("tensor", sG, wk_done)
        wins = [(0, 259), (259, 256)]
        gi = 0
        for dt_i in range(35):
            M = 128 if dt_i < 34 else 32
            for (w0, nw) in wins:
                bk = (0, 2, 3, 4)[gi % 4]
                gi += 1
                bank_wait("tensor", [bk])
                for k in range(8):
                    lhs = winT[:, k * DIN + 128 * dt_i: k * DIN + 128 * dt_i + M]
                    rhs = h1fm[:, k * WROW + w0: k * WROW + w0 + nw]
                    p.op("tensor", lambda e, o=PS[bk][0:M, 0:nw], l=lhs, r=rhs,
                         s0=(k == 0), s1=(k == 7): e.matmul(o, l, r, start=s0, stop=s1),
                         st if k == 7 else None)
                if dt_i < 16:
                    dst = z_fm[:, dt_i * 515 + w0: dt_i * 515 + w0 + nw]
                    p.wait("vector", st, st.n)
                    p.op("vector", lambda e, ds=dst, bk=bk, nw=nw: e.tensor_copy(
                        ds, PS[bk][:, 0:nw]), sv)
                    mark_bank(bk, sv)
                elif dt_i < 34:
                    ct = dt_i - 16
                    dst = xbc_fm[:, ct * 515 + w0: ct * 515 + w0 + nw]
                    p.wait("scalar", st, st.n)
                    p.op("scalar", lambda e, ds=dst, bk=bk, nw=nw: e.copy(
                        ds, PS[bk][:, 0:nw]), ss)
                    mark_bank(bk, ss)
                else:
                    dst = dtraw[:, w0:w0 + nw]
                    p.wait("scalar", st, st.n)
                    p.op("scalar", lambda e, ds=dst, bk=bk, nw=nw: e.activation(
                        ds, PS[bk][0:32, 0:nw], AF.Identity, bias=dtb_s[:, 0:1]), ss)
                    mark_bank(bk, ss)
        a1_v, a1_s = sv.n, ss.n
        p.dma("scalar", dtsp.ap()[b], dtraw[:], sS)
        p.wait("vector", ss, a1_s)
        p.wait("vector", sv, a1_v)
        conv_silu_cnt = [0] * 18
        conv_spill_cnt = [0] * 18
        for ct in range(18):
            ca, cs_ = (cacc, csil) if ct % 2 == 0 else (caccB, csilB)
            base = ct * 515
            bc = bconv_s[:, ct:ct + 1].broadcast_to((128, W))
            if ct >= 2:
                p.wait("vector", ss, conv_silu_cnt[ct - 2])
            p.op("vector", lambda e, ca=ca, ct=ct, b0=base, bc=bc: e.scalar_tensor_tensor(
                ca[:], xbc_fm[:, b0:b0 + W], wconv_s[:, ct * KC:ct * KC + 1], bc,
                ALU.mult, ALU.add), sv)
            for k in range(1, KC):
                p.op("vector", lambda e, ca=ca, ct=ct, b0=base, k=k: e.scalar_tensor_tensor(
                    ca[:], xbc_fm[:, b0 + k:b0 + k + W],
                    wconv_s[:, ct * KC + k:ct * KC + k + 1], ca[:],
                    ALU.mult, ALU.add), sv)
            if ct >= 2:
                p.wait("scalar", sS, conv_spill_cnt[ct - 2])
            p.wait("scalar", sv, sv.n)
            p.op("scalar", lambda e, ca=ca, cs_=cs_: e.activation(cs_[:], ca[:], AF.Silu), ss)
            conv_silu_cnt[ct] = ss.n
            if ct < 16:
                dst = xpsp.ap()[b, :, ct, :]
            elif ct == 16:
                dst = bsp.ap()[b]
            else:
                dst = csp.ap()[b]
            p.wait("scalar", ss, ss.n)
            p.dma("scalar", dst, cs_[:], sS)
            conv_spill_cnt[ct] = sS.n
        z_spill_cnt = [0] * 16
        for zt in range(16):
            cz = csilzA if zt % 2 == 0 else csilzB
            if zt >= 2:
                p.wait("scalar", sS, z_spill_cnt[zt - 2])
            src = z_fm[:, zt * 515 + 3: zt * 515 + 3 + W]
            p.op("scalar", lambda e, cz=cz, s=src: e.activation(cz[:], s, AF.Silu), ss)
            p.wait("scalar", ss, ss.n)
            p.dma("scalar", szsp.ap()[b, :, zt, :], cz[:], sS)
            z_spill_cnt[zt] = sS.n

    sb.open_phase()
    xt = sb.ph("xt", [128, 4 * HP], BF16)
    Bt = sb.ph("Bt", [128, 4 * 128], BF16)
    Bfm = sb.ph("Bfm", [128, W], BF16)
    Cfm = sb.ph("Cfm", [128, W], BF16)
    dtraw2 = sb.ph("dtraw2", [32, 515], F32)
    u32 = sb.ph("u32", [32, W], F32)
    dtv = sb.ph("dtv", [32, W], F32)
    dAv = sb.ph("dAv", [32, W], F32)
    dAb = sb.ph("dAb", [32, W], BF16)
    csT = sb.ph("csT", [32, 128], F32)
    csTb = sb.ph("csTb", [32, 128], BF16)
    dtvb = sb.ph("dtvb", [32, 128], BF16)
    csDb = sb.ph("csDb", [32, 128], BF16)
    decayT = sb.ph("decayT", [32, 128], F32)
    ddtTb = sb.ph("ddtTb", [32, 128], BF16)
    dcol = sb.ph("dcol", [32, 1], F32)
    rhs32b = sb.ph("rhs32b", [32, 32], BF16)
    logD0 = sb.ph("logD0", [32, 1], F32)
    ddtS = sb.ph("ddtS", [128, 32], BF16)
    dtS = sb.ph("dtS", [128, 32], BF16)
    dAbS = sb.ph("dAbS", [128, 32], BF16)
    expcsS = sb.ph("expcsS", [128, 32], F32)
    decrep = sb.ph("decrep", [128, 32], F32)
    rhs32 = sb.ph("rhs32", [32, 32], F32)
    S0m = sb.ph("S0m", [128, 128], BF16)
    xtd = sb.ph("xtd", [128, HP], BF16)
    xdd = sb.ph("xdd", [128, HP], BF16)
    damask = sb.ph("damask", [128, 1024], BF16)
    dq = sb.ph("dq", [128, 1024], BF16)
    lmq = sb.ph("lmq", [128, 1024], BF16)
    scq = sb.ph("scq", [128, 1024], BF16)
    damaskB = sb.ph("damaskB", [128, 1024], BF16)
    dqB = sb.ph("dqB", [128, 1024], BF16)
    lmqB = sb.ph("lmqB", [128, 1024], BF16)
    scqB = sb.ph("scqB", [128, 1024], BF16)
    ysb = sb.ph("ysb", [128, HP], BF16)
    yofft = sb.ph("yofft", [128, 1024], F32)
    dtmp = sb.ph("dtmp", [128, HP], BF16)
    S_run = sb.ph("S_run", [128, HP], F32)
    Srun_bf = sb.ph("Srun_bf", [128, HP], BF16)

    fence()
    for b in range(Bb):
        p.phase(10)
        p.wait("sync", sv, sv.n)
        p.wait("sync", st, st.n)
        for ct in range(16):
            p.dmat("sync", xt[:].rearrange("p (c hp) -> p c hp", c=4)
                   [:, :, 128 * ct:128 * (ct + 1)], xpsp.ap()[b, :, ct, :], sL)
        p.dmat("sync", Bt[:].rearrange("p (c n) -> p c n", c=4), bsp.ap()[b], sL)
        p.dma("sync", Bfm[:], bsp.ap()[b], sL)
        p.dma("sync", Cfm[:], csp.ap()[b], sL)
        p.dma("sync", dtraw2[:], dtsp.ap()[b], sL)
        b_ld = sL.n
        p.wait("scalar", sL, b_ld)
        p.op("scalar", lambda e: e.activation(u32[:], dtraw2[:, 3:515], AF.Exp), ss)
        p.op("scalar", lambda e: e.activation(dtv[:], u32[:], AF.Ln, bias=1.0), ss)
        p.wait("vector", ss, ss.n)
        p.op("vector", lambda e: e.tensor_scalar_mul(dAv[:], dtv[:], acol_s[:, 0:1]), sv)
        p.op("vector", lambda e: e.tensor_copy(dAb[:], dAv[:]), sv)
        p.op("vector", lambda e, b=b: e.reduce_sum(dtot_s[:, b:b + 1], dAv[:], AX.X), sv)
        p.wait("vector", sS, sS.n)
        p.op("vector", lambda e: e.tensor_copy(
            logD0[:], c_zero[0:32, 0:1]), sv)
        p.op("vector", lambda e: e.tensor_copy(
            S_run[:], c_zero[:, 0:1].broadcast_to((128, HP))), sv)
        p.op("vector", lambda e: e.tensor_copy(
            Srun_bf[:], c_zero[:, 0:1].broadcast_to((128, HP))), sv)
        srun_ready = 0
        dt_done = sv.n
        p.wait("tensor", sL, b_ld)

        for c in range(4):
            p.phase(11)
            csl = slice(128 * c, 128 * (c + 1))
            p.wait("vector", st, st.n)
            p.wait("vector", ss, ss.n)
            p.wait("vector", sS, sS.n)
            p.wait("vector", sv, dt_done)
            p.op("vector", lambda e, c=c: e.tensor_tensor_scan(
                csT[:], c_ones32f[:], dAv[:, 128 * c:128 * (c + 1)], 0.0,
                ALU.mult, ALU.add), sv)
            p.op("vector", lambda e: e.tensor_copy(csTb[:], csT[:]), sv)
            p.op("vector", lambda e, c=c: e.tensor_copy(
                dtvb[:], dtv[:, 128 * c:128 * (c + 1)]), sv)
            p.op("vector", lambda e: e.tensor_scalar(
                csDb[:], csT[:], logD0[:, 0:1], None, ALU.add), sv)
            p.op("vector", lambda e: e.tensor_scalar(
                logD0[:], logD0[:], csT[:, 127:128], None, ALU.add), sv)
            p.wait("scalar", sv, sv.n)
            p.op("scalar", lambda e: e.activation(
                decayT[:], csT[:], AF.Exp, scale=-1.0, bias=csT[:, 127:128]), ss)
            p.op("scalar", lambda e: e.activation(
                dcol[:], csT[:, 127:128], AF.Exp), ss)
            p.wait("vector", ss, ss.n)
            TT("vector", ddtTb[:], decayT[:], dtv[:, csl], ALU.mult, sv)
            p.op("vector", lambda e: e.tensor_tensor(
                rhs32b[:], dcol[:, 0:1].broadcast_to((32, 32)), c_eye32b[:],
                ALU.mult), sv)
            p.wait("tensor", sv, sv.n)
            bank_wait("tensor", [1])
            p.op("tensor", lambda e: e.transpose(PS[1][:, 0:32], ddtTb[:], c_eye32b[:]), st)
            p.op("tensor", lambda e: e.transpose(PS[1][:, 32:64], dtvb[:], c_eye32b[:]), st)
            p.op("tensor", lambda e: e.transpose(PS[1][:, 64:96], csTb[:], c_eye32b[:]), st)
            p.op("tensor", lambda e: e.transpose(PS[1][:, 96:128], csDb[:], c_eye32b[:]), st)
            p.op("tensor", lambda e, c=c: e.transpose(
                PS[1][:, 128:160], dAb[:, 128 * c:128 * (c + 1)], c_eye32b[:]), st)
            bank_wait("tensor", [0])
            p.op("tensor", lambda e: e.matmul(
                PS[0][:, 288:320], c_o32x128b[:], rhs32b[:], start=True, stop=True), st)
            p.phase(12)
            p.op("tensor", lambda e, c=c: e.matmul(
                PS[0][:, 0:128], Bfm[:, 128 * c:128 * (c + 1)],
                Cfm[:, 128 * c:128 * (c + 1)], start=True, stop=True), st)
            tr_done = st.n
            p.wait("vector", st, tr_done)
            p.op("vector", lambda e: e.tensor_copy(ddtS[:], PS[1][:, 0:32]), sv)
            p.op("vector", lambda e: e.tensor_copy(dtS[:], PS[1][:, 32:64]), sv)
            p.op("vector", lambda e: e.tensor_copy(dAbS[:], PS[1][:, 128:160]), sv)
            mark_bank(1, sv)
            p.wait("scalar", st, tr_done)
            p.op("scalar", lambda e: e.activation(expcsS[:], PS[1][:, 64:96], AF.Exp), ss)
            p.op("scalar", lambda e, b=b, c=c: e.activation(
                expcsD[:, (b * 4 + c) * 32:(b * 4 + c + 1) * 32], PS[1][:, 96:128],
                AF.Exp), ss)
            p.op("scalar", lambda e: e.copy(decrep[:], PS[0][:, 288:320]), ss)
            mark_bank(1, ss)
            mark_bank(0, ss)
            p.wait("vector", st, st.n)
            TT("vector", S0m[:], PS[0][:, 0:128], c_trilT[:], ALU.mult, sv)
            mark_bank(0, sv)
            p.phase(13)
            xt_c = xt[:, HP * c: HP * (c + 1)]
            xt3 = xt_c.rearrange("p (h q) -> p h q", h=NH)
            TT("vector", xtd[:].rearrange("p (h q) -> p h q", h=NH), xt3, rep_h(dtS[:]),
               ALU.mult, sv)
            TT("vector", xdd[:].rearrange("p (h q) -> p h q", h=NH), xt3, rep_h(ddtS[:]),
               ALU.mult, sv)
            p.phase(14)
            q_mm_cnt = [0] * 4
            q_exp_cnt = [0] * 4
            for qh in range(4):
                dmk, dqq, lmk, sck = ((damask, dq, lmq, scq) if qh % 2 == 0
                                      else (damaskB, dqB, lmqB, scqB))
                hsl = slice(8 * qh, 8 * (qh + 1))
                dabv = dAbS[:, hsl].unsqueeze(2).broadcast_to((128, 8, 128))
                tri = c_triu[:].unsqueeze(1).broadcast_to((128, 8, 128))
                if qh >= 2:
                    p.wait("vector", ss, q_exp_cnt[qh - 2])
                    p.wait("vector", st, q_mm_cnt[qh - 2])
                TT("vector", dmk[:].rearrange("p (h q) -> p h q", h=8), dabv, tri,
                   ALU.mult, sv)
                bank_wait("tensor", [2, 3])
                p.wait("tensor", sv, sv.n)
                for half in range(2):
                    bk = 2 + half
                    rv = dabv[:, 4 * half:4 * (half + 1), :]
                    mv = dmk[:, 512 * half:512 * (half + 1)]
                    p.op("tensor", lambda e, bk=bk, rv=rv: e.matmul(
                        PS[bk][:, 0:512], c_negleq[:], rv, start=True, stop=False))
                    p.op("tensor", lambda e, bk=bk, mv=mv: e.matmul(
                        PS[bk][:, 0:512], c_all1[:], mv, start=False, stop=True), st)
                p.wait("vector", st, st.n)
                for half in range(2):
                    p.op("vector", lambda e, dqq=dqq, half=half: e.tensor_scalar_min(
                        dqq[:, 512 * half:512 * (half + 1)],
                        PS[2 + half][:, 0:512], 0.0), sv)
                mark_bank(2, sv)
                mark_bank(3, sv)
                p.wait("scalar", sv, sv.n)
                p.op("scalar", lambda e, dqq=dqq, lmk=lmk: e.activation(
                    lmk[:], dqq[:], AF.Exp), ss)
                q_exp_cnt[qh] = ss.n
                p.wait("vector", ss, ss.n)
                s0v = S0m[:].unsqueeze(1).broadcast_to((128, 8, 128))
                TT("vector", sck[:].rearrange("p (h q) -> p h q", h=8), lmk[:]
                   .rearrange("p (h q) -> p h q", h=8), s0v, ALU.mult, sv)
                bank_wait("tensor", [4 + qh])
                p.wait("tensor", sv, sv.n)
                for hh in range(8):
                    h = 8 * qh + hh
                    p.op("tensor", lambda e, sck=sck, qh=qh, hh=hh, h=h: e.matmul(
                        PS[4 + qh][:, 64 * hh:64 * (hh + 1)],
                        sck[:, 128 * hh:128 * (hh + 1)],
                        xtd[:, 64 * h: 64 * (h + 1)],
                        start=True, stop=True), st if hh == 7 else None)
                q_mm_cnt[qh] = st.n
            p.phase(15)
            p.wait("vector", st, st.n)
            for qh in range(4):
                p.op("vector", lambda e, qh=qh: e.tensor_copy(
                    ysb[:, 512 * qh:512 * (qh + 1)], PS[4 + qh][:, 0:512]), sv)
                mark_bank(4 + qh, sv)
            if c > 0:
                p.wait("tensor", ss, srun_ready)
                for half in range(2):
                    bank_wait("tensor", [2, 3])
                    p.wait("tensor", sv, sv.n)
                    for hh in range(16):
                        h = 16 * half + hh
                        bk = 2 + hh // 8
                        p.op("tensor", lambda e, bk=bk, hh=hh, h=h, c=c: e.matmul(
                            PS[bk][:, 64 * (hh % 8):64 * (hh % 8 + 1)],
                            Cfm[:, 128 * c:128 * (c + 1)],
                            Srun_bf[:, 64 * h:64 * (h + 1)],
                            start=True, stop=True), st if hh == 15 else None)
                    p.wait("scalar", st, st.n)
                    p.wait("scalar", sv, sv.n)
                    for hh in range(16):
                        h = 16 * half + hh
                        bk = 2 + hh // 8
                        p.op("scalar", lambda e, bk=bk, hh=hh, h=h: e.activation(
                            yofft[:, 64 * hh:64 * (hh + 1)],
                            PS[bk][:, 64 * (hh % 8):64 * (hh % 8 + 1)],
                            AF.Identity, scale=expcsS[:, h:h + 1]), ss)
                    mark_bank(2, ss)
                    mark_bank(3, ss)
                    p.wait("vector", ss, ss.n)
                    TT("vector", ysb[:, 1024 * half:1024 * (half + 1)],
                       ysb[:, 1024 * half:1024 * (half + 1)], yofft[:], ALU.add, sv)
            TT("vector", dtmp[:], xt_c, dfull_s[:], ALU.mult, sv)
            TT("vector", ysb[:], ysb[:], dtmp[:], ALU.add, sv)
            y_done = sv.n
            p.wait("scalar", sv, y_done)
            p.dma("scalar", ysp.ap()[b, c], ysb[:], sS)
            p.phase(16)
            bank_wait("tensor", [4, 5, 6, 7])
            p.wait("tensor", sv, sv.n)
            for i in range(4):
                p.op("tensor", lambda e, i=i, c=c: e.matmul(
                    PS[4 + i][:, 0:512], Bt[:, 128 * c:128 * (c + 1)],
                    xdd[:, 512 * i:512 * (i + 1)], start=True, stop=True),
                    st if i == 3 else None)
            p.wait("vector", st, st.n)
            p.wait("vector", ss, ss.n)
            for h in range(NH):
                bk = 4 + h // 8
                p.op("vector", lambda e, h=h, bk=bk: e.scalar_tensor_tensor(
                    S_run[:, 64 * h:64 * (h + 1)], S_run[:, 64 * h:64 * (h + 1)],
                    decrep[:, h:h + 1], PS[bk][:, 64 * (h % 8):64 * (h % 8 + 1)],
                    ALU.mult, ALU.add), sv)
            for i in range(4):
                mark_bank(4 + i, sv)
            p.wait("scalar", sv, sv.n)
            p.op("scalar", lambda e: e.copy(Srun_bf[:], S_run[:]), ss)
            srun_ready = ss.n
        p.phase(19)
        p.wait("gpsimd", ss, ss.n)
        p.dma("gpsimd", gin.ap()[b], Srun_bf[:], sG)
    p.phase(20)
    p.wait("gpsimd", sv, sv.n)
    p.dma("gpsimd", gin2.ap(), dtot_s[:], sG)
    p.wait("gpsimd", sS, sS.n)
    p.op("gpsimd", lambda e: e.collective_compute(
        "AllGather", ALU.bypass, [[0, 1, 2, 3, 4, 5, 6, 7]],
        [gin.ap()], [gout.ap()]), sGC)
    p.op("gpsimd", lambda e: e.collective_compute(
        "AllGather", ALU.bypass, [[0, 1, 2, 3, 4, 5, 6, 7]],
        [gin2.ap()], [gout2.ap()]), sGC)

    sb.open_phase()
    wgT_s = sb.ph("wgT", [128, 8 * 3 * H], BF16)
    ugT_s = sb.ph("ugT", [128, 8 * 3 * H], BF16)
    cst = sb.ph("cst", [128, 8 * W], F32)
    cbf = sb.ph("cbf", [128, 8 * W], BF16)
    xnfm = [sb.ph("xnfm", [128, 8 * W], BF16) for _ in range(2)]
    gbuf = sb.ph("gbuf", [128, 3 * W], BF16)
    tt1 = sb.ph("tt1", [128, W], F32)
    tt2 = sb.ph("tt2", [128, W], F32)

    p.phase(21)
    fence()
    p.dma("gpsimd", wgT_s[:].rearrange("p (k m) -> p k m", k=8),
          d["wgT"].rearrange("k p m -> p k m"), sG)
    p.dma("gpsimd", ugT_s[:].rearrange("p (k m) -> p k m", k=8),
          d["ugT"].rearrange("k p m -> p k m"), sG)
    wg_done = sG.n
    p.wait("vector", sS, sS.n)
    for ft in range(8):
        p.op("vector", lambda e, ft=ft: e.tensor_copy(
            cst[:, W * ft:W * (ft + 1)],
            inc0_s[:, ft:ft + 1].broadcast_to((128, W))), sv)
    for t in range(Bb):
        buf = xnfm[t % 2]
        p.wait("sync", st, st.n)
        p.dmat("sync", buf[:].rearrange("p (k t) -> p k t", k=8),
               xnsp.ap()[t, 3:515, :], sL)
        xn_ld = sL.n
        p.wait("vector", st, st.n)
        for ft in range(8):
            p.op("vector", lambda e, ft=ft: e.tensor_copy(
                cbf[:, W * ft:W * (ft + 1)], cst[:, W * ft:W * (ft + 1)]), sv)
        cb_done = sv.n
        p.wait("tensor", sv, cb_done)
        p.wait("tensor", sL, xn_ld)
        if t == 0:
            p.wait("tensor", sG, wg_done)
        for ft in range(8):
            for part in range(3):
                mcol = part * H + 128 * ft
                bk = (0, 2, 3)[(ft * 3 + part) % 3]
                bank_wait("tensor", [bk])
                for k in range(8):
                    p.op("tensor", lambda e, bk=bk, k=k, mcol=mcol, buf=buf: e.matmul(
                        PS[bk][:, 0:512], wgT_s[:, k * 3 * H + mcol:k * 3 * H + mcol + 128],
                        buf[:, W * k:W * (k + 1)], start=(k == 0), stop=False))
                for k in range(8):
                    p.op("tensor", lambda e, bk=bk, k=k, mcol=mcol: e.matmul(
                        PS[bk][:, 0:512], ugT_s[:, k * 3 * H + mcol:k * 3 * H + mcol + 128],
                        cbf[:, W * k:W * (k + 1)], start=False, stop=(k == 7)),
                        st if k == 7 else None)
                fn = AF.Sigmoid if part < 2 else AF.Tanh
                p.wait("scalar", st, st.n)
                p.wait("scalar", sv, sv.n)
                p.op("scalar", lambda e, bk=bk, part=part, ft=ft, fn=fn: e.activation(
                    gbuf[:, W * part:W * (part + 1)], PS[bk][:, 0:512], fn,
                    bias=bg_s[:, part * 8 + ft:part * 8 + ft + 1]), ss)
                mark_bank(bk, ss)
            p.wait("vector", ss, ss.n)
            TT("vector", tt1[:], gbuf[:, W:2 * W], gbuf[:, 2 * W:3 * W], ALU.mult, sv)
            TT("vector", tt2[:], gbuf[:, 0:W], cst[:, W * ft:W * (ft + 1)], ALU.mult, sv)
            TT("vector", cst[:, W * ft:W * (ft + 1)], tt1[:], tt2[:], ALU.add, sv)
    p.wait("vector", st, st.n)
    cfin = sv.n
    sb.open_phase()
    woT_sb = sb.ph("woT", [128, 8 * H], BF16)
    uoT_sb = sb.ph("uoT", [128, 8 * H], BF16)
    cbf2 = sb.ph("cbf2", [128, 8 * W], BF16)
    cuof = sb.ph("cuof", [128, 8 * W], BF16)
    cuot = sb.ph("cuot", [128, 4 * H], BF16)
    xnfm2 = [sb.ph("xnfm2", [128, 8 * W], BF16) for _ in range(2)]
    htmp = sb.ph("htmp", [128, H], F32)
    hbuf = sb.ph("hbuf", [128, H], BF16)

    fence()
    p.wait("vector", sv, cfin)
    for ft in range(8):
        p.op("vector", lambda e, ft=ft: e.tensor_copy(
            cbf2[:, W * ft:W * (ft + 1)], cst[:, W * ft:W * (ft + 1)]), sv)
    cfin2 = sv.n
    p.wait("gpsimd", sv, cfin2)
    p.dma("gpsimd", woT_sb[:].rearrange("p (k m) -> p k m", k=8),
          d["woT"].rearrange("k p m -> p k m"), sG)
    p.dma("gpsimd", uoT_sb[:].rearrange("p (k m) -> p k m", k=8),
          d["uoT"].rearrange("k p m -> p k m"), sG)
    wo_done = sG.n
    p.wait("tensor", sG, wo_done)
    for ot in range(8):
        bk = (0, 2)[ot % 2]
        bank_wait("tensor", [bk])
        for k in range(8):
            p.op("tensor", lambda e, bk=bk, k=k, ot=ot: e.matmul(
                PS[bk][:, 0:512], uoT_sb[:, k * H + 128 * ot:k * H + 128 * ot + 128],
                cbf2[:, W * k:W * (k + 1)], start=(k == 0), stop=(k == 7)),
                st if k == 7 else None)
        p.wait("scalar", st, st.n)
        p.op("scalar", lambda e, bk=bk, ot=ot: e.activation(
            cuof[:, W * ot:W * (ot + 1)], PS[bk][:, 0:512], AF.Identity,
            bias=bo_s[:, ot:ot + 1]), ss)
        mark_bank(bk, ss)
    p.wait("sync", ss, ss.n)
    for ot in range(8):
        p.dmat("sync", cuot[:].rearrange("p (tt o) -> p tt o", tt=4)
               [:, :, 128 * ot:128 * (ot + 1)], cuof[:, W * ot:W * (ot + 1)], sL)
    cuot_ld = sL.n
    for t in range(Bb):
        buf = xnfm2[t % 2]
        p.wait("sync", st, st.n)
        p.dmat("sync", buf[:].rearrange("p (k t) -> p k t", k=8),
               xnsp.ap()[t, 3:515, :], sL)
        xn_ld = sL.n
        p.wait("tensor", sL, max(xn_ld, cuot_ld))
        for tt in range(4):
            bank_wait("tensor", [0, 2])
            for half in range(2):
                for k in range(8):
                    p.op("tensor", lambda e, half=half, k=k, tt=tt, buf=buf: e.matmul(
                        PS[(0, 2)[half]][:, 0:512],
                        buf[:, W * k + 128 * tt:W * k + 128 * (tt + 1)],
                        woT_sb[:, k * H + 512 * half:k * H + 512 * (half + 1)],
                        start=(k == 0), stop=(k == 7)),
                        st if (k == 7 and half == 1) else None)
            p.wait("vector", st, st.n)
            for half in range(2):
                TT("vector", htmp[:, 512 * half:512 * (half + 1)],
                   PS[(0, 2)[half]][:, 0:512],
                   cuot[:, H * tt + 512 * half:H * tt + 512 * (half + 1)], ALU.add, sv)
            mark_bank(0, sv)
            mark_bank(2, sv)
            p.wait("scalar", sv, sv.n)
            p.op("scalar", lambda e: e.activation(hbuf[:], htmp[:], AF.Tanh), ss)
            p.wait("scalar", ss, ss.n)
            p.dma("scalar", hsp.ap()[t, tt], hbuf[:], sS)
            p.wait("vector", sS, sS.n)

    sb.open_phase()
    woutT = sb.ph("woutT", [128, 16 * H], BF16)
    g2_s = sb.ph("g2s", [32, 64], F32)
    wacc = sb.ph("wacc", [32, 64], F32)
    mixc_s = sb.ph("mixc", [32, 64], F32)
    mski_s = sb.ph("mski", [32, 64], F32)
    wtb = sb.ph("wtb", [32, 32], BF16)
    stile = [sb.ph("stile", [128, HP], BF16) for _ in range(2)]
    wcb = sb.ph("wcb", [128, 32], F32)
    accS = sb.ph("accS", [128, HP], F32)
    tmpb = sb.ph("tmpb", [128, HP], BF16)
    sinb = sb.ph("sinb", [128, HP], BF16)
    yld = sb.ph("yld", [128, HP], BF16)
    szt = sb.ph("szt", [128, 4 * HP], BF16)
    Cfm2 = sb.ph("Cfm2", [128, W], BF16)
    statb = sb.ph("statb", [128, 4], F32)
    xtile = sb.ph("xtile", [128, H], F32)
    yofft2 = sb.ph("yofft2", [128, 1024], F32)
    g1 = sb.ph("g1", [128, HP], BF16)
    gg = sb.ph("gg", [128, HP], BF16)
    sq2 = sb.ph("sq2", [128, HP], F32)
    nrm = sb.ph("nrm", [128, 8], F32)
    gn = sb.ph("gn", [128, HP], BF16)
    gfm = sb.ph("gfm", [128, 16 * 128], BF16)
    xnt = sb.ph("xnt", [128, H], F32)
    hid = sb.ph("hid", [128, H], F32)
    h2b = sb.ph("h2b", [128, H], BF16)

    p.phase(22)
    fence()
    p.dma("gpsimd", woutT[:].rearrange("p (k m) -> p k m", k=16),
          d["w_outT"].rearrange("k p m -> p k m"), sG)
    wout_done = sG.n
    p.wait("vector", sG, wout_done)
    for k in range(16):
        sl = woutT[:, k * H:(k + 1) * H]
        p.op("vector", lambda e, s=sl, k=k: e.tensor_scalar_mul(
            s, s, gammn_s[:, k:k + 1]), sv)
    wout_fold = sv.n
    p.wait("sync", sGC, sGC.n)
    p.dma("sync", g2_s[:].rearrange("h (k b) -> h k b", k=8),
          gout2.ap().rearrange("k h b -> h k b"), sL)
    p.dma("sync", mixc_s[:], d["mixcol"], sL)
    p.dma("sync", mski_s[:], d["maskinit"], sL)
    p.wait("vector", sL, sL.n)
    p.op("vector", lambda e: e.tensor_copy(wacc[:], mski_s[:]), sv)
    for i in range(8):
        for k in range(8):
            p.op("vector", lambda e, i=i, k=k: e.scalar_tensor_tensor(
                wacc[:, 8 * i:8 * (i + 1)], g2_s[:, 8 * k:8 * (k + 1)],
                mixc_s[:, 8 * i + k:8 * i + k + 1], wacc[:, 8 * i:8 * (i + 1)],
                ALU.mult, ALU.add), sv)
    p.wait("scalar", sv, sv.n)
    p.op("scalar", lambda e: e.activation(wt_s[:], wacc[:], AF.Exp), ss)
    wexp_done = ss.n
    if _dbg:
        p.wait("sync", sv, sv.n)
        p.wait("sync", ss, ss.n)
        p.dma("sync", dbgw.ap()[0], g2_s[:], sL)
        p.dma("sync", dbgw.ap()[1], wacc[:], sL)
        p.dma("sync", dbgw.ap()[2], wt_s[:], sL)

    for b in range(Bb):
        p.wait("vector", ss, ss.n)
        p.wait("vector", ss, wexp_done)
        p.op("vector", lambda e: e.tensor_copy(
            accS[:], c_zero[:, 0:1].broadcast_to((128, HP))), sv)
        for i in range(NCORE):
            bufS = stile[i % 2]
            p.wait("sync", sv, sv.n)
            p.dma("sync", bufS[:], gout.ap()[i, b], sL)
            s_ld = sL.n
            cidx = i * 8 + b
            p.op("vector", lambda e, cidx=cidx: e.tensor_tensor(
                wtb[:], wt_s[:, cidx:cidx + 1].broadcast_to((32, 32)),
                c_eye32b[:], ALU.mult), sv)
            p.wait("tensor", sv, sv.n)
            bank_wait("tensor", [0])
            p.op("tensor", lambda e: e.matmul(
                PS[0][:, 320:352], c_o32x128b[:], wtb[:], start=True, stop=True), st)
            p.wait("vector", st, st.n)
            p.op("vector", lambda e: e.tensor_copy(wcb[:], PS[0][:, 320:352]), sv)
            mark_bank(0, sv)
            p.wait("vector", sL, s_ld)
            TT("vector", tmpb[:].rearrange("p (h q) -> p h q", h=NH),
               bufS[:].rearrange("p (h q) -> p h q", h=NH), rep_h(wcb[:]), ALU.mult, sv)
            TT("vector", accS[:], accS[:], tmpb[:], ALU.add, sv)
        p.wait("scalar", sv, sv.n)
        p.wait("scalar", st, st.n)
        p.op("scalar", lambda e: e.copy(sinb[:], accS[:]), ss)
        sin_done = ss.n
        if _dbg:
            p.wait("sync", ss, ss.n)
            p.dma("sync", dbgsin.ap()[b], sinb[:], sL)
        p.wait("sync", sv, sv.n)
        p.wait("sync", st, st.n)
        for ct in range(16):
            p.dmat("sync", szt[:].rearrange("p (c hp) -> p c hp", c=4)
                   [:, :, 128 * ct:128 * (ct + 1)], szsp.ap()[b, :, ct, :], sL)
        p.dma("sync", Cfm2[:], csp.ap()[b], sL)
        szld = sL.n
        for c in range(4):
            p.wait("sync", sv, sv.n)
            p.wait("sync", ss, ss.n)
            p.wait("vector", sL, sL.n)
            p.dma("sync", yld[:], ysp.ap()[b, c], sL)
            p.dma("sync", statb[:], statsp.ap()[b, 3 + 128 * c:131 + 128 * c, :], sL)
            p.dma("sync", xtile[:], d["xin"][b, 3 + 128 * c:131 + 128 * c, :], sL)
            c_ld = sL.n
            ecol = expcsD[:, (b * 4 + c) * 32:(b * 4 + c + 1) * 32]
            for half in range(2):
                bank_wait("tensor", [2, 3])
                p.wait("tensor", ss, sin_done)
                p.wait("tensor", sL, szld)
                for hh in range(16):
                    h = 16 * half + hh
                    bk = 2 + hh // 8
                    p.op("tensor", lambda e, bk=bk, hh=hh, h=h, c=c: e.matmul(
                        PS[bk][:, 64 * (hh % 8):64 * (hh % 8 + 1)],
                        Cfm2[:, 128 * c:128 * (c + 1)],
                        sinb[:, 64 * h:64 * (h + 1)], start=True, stop=True),
                        st if hh == 15 else None)
                p.wait("scalar", st, st.n)
                p.wait("scalar", sv, sv.n)
                for hh in range(16):
                    h = 16 * half + hh
                    bk = 2 + hh // 8
                    p.op("scalar", lambda e, bk=bk, hh=hh, h=h, ec=ecol: e.activation(
                        yofft2[:, 64 * hh:64 * (hh + 1)],
                        PS[bk][:, 64 * (hh % 8):64 * (hh % 8 + 1)],
                        AF.Identity, scale=ec[:, h:h + 1]), ss)
                mark_bank(2, ss)
                mark_bank(3, ss)
                p.wait("vector", ss, ss.n)
                p.wait("vector", sL, c_ld)
                TT("vector", g1[:, 1024 * half:1024 * (half + 1)],
                   yld[:, 1024 * half:1024 * (half + 1)], yofft2[:], ALU.add, sv)
            TT("vector", gg[:], g1[:], szt[:, HP * c:HP * (c + 1)], ALU.mult, sv)
            p.wait("scalar", sv, sv.n)
            p.op("scalar", lambda e: e.activation(
                sq2[:], gg[:], AF.Square, accum_out=nrm[:, 0:1]), ss)
            p.op("scalar", lambda e: e.activation(
                nrm[:, 1:2], nrm[:, 0:1], AF.Ln, scale=1.0 / HP, bias=NEPS), ss)
            p.op("scalar", lambda e: e.activation(
                nrm[:, 2:3], nrm[:, 1:2], AF.Exp, scale=-0.5), ss)
            p.wait("vector", ss, ss.n)
            p.op("vector", lambda e: e.tensor_scalar_mul(gn[:], gg[:], nrm[:, 2:3]), sv)
            p.wait("sync", sv, sv.n)
            if _dbg:
                p.dma("sync", dbgg.ap()[b, c], gg[:], sL)
                p.dma("sync", dbgn.ap()[b, c], nrm[:], sL)
            p.dmat("sync", gfm[:].rearrange("p (ct q) -> p ct q", ct=16), gn[:], sL)
            gfm_ld = sL.n
            p.wait("tensor", sL, gfm_ld)
            p.wait("tensor", sv, wout_fold)
            bank_wait("tensor", [4, 5])
            for half in range(2):
                for k in range(16):
                    p.op("tensor", lambda e, half=half, k=k: e.matmul(
                        PS[4 + half][:, 0:512], gfm[:, 128 * k:128 * (k + 1)],
                        woutT[:, k * H + 512 * half:k * H + 512 * (half + 1)],
                        start=(k == 0), stop=(k == 15)),
                        st if (k == 15 and half == 1) else None)
            p.op("vector", lambda e: e.tensor_scalar(
                xnt[:], xtile[:], statb[:, 0:1], statb[:, 1:2],
                ALU.mult, ALU.subtract), sv)
            p.wait("vector", st, st.n)
            for half in range(2):
                TT("vector", hid[:, 512 * half:512 * (half + 1)],
                   xnt[:, 512 * half:512 * (half + 1)], PS[4 + half][:, 0:512],
                   ALU.add, sv)
            mark_bank(4, sv)
            mark_bank(5, sv)
            p.wait("scalar", sv, sv.n)
            p.dma("scalar", hidsp.ap()[b, c], hid[:], sS)
            p.op("scalar", lambda e: e.activation(
                sq2[:, 0:H], hid[:], AF.Square, accum_out=nrm[:, 3:4]), ss)
            p.op("scalar", lambda e: e.activation(
                nrm[:, 4:5], nrm[:, 3:4], AF.Ln, scale=1.0 / H, bias=EPS), ss)
            p.op("scalar", lambda e: e.activation(
                nrm[:, 5:6], nrm[:, 4:5], AF.Exp, scale=-0.5), ss)
            p.wait("vector", ss, ss.n)
            p.op("vector", lambda e: e.tensor_scalar_mul(h2b[:], hid[:], nrm[:, 5:6]), sv)
            p.wait("scalar", sv, sv.n)
            p.dma("scalar", h2sp.ap()[b, c], h2b[:], sS)
            p.wait("vector", sS, sS.n)

    sb.open_phase()
    wgc = sb.ph("wgc", [128, 8 * 1024], BF16)
    wuc = sb.ph("wuc", [128, 8 * 1024], BF16)
    wdc = sb.ph("wdc", [128, 8 * 1024], BF16)
    h2fm = sb.ph("h2fm", [128, 8 * W], BF16)
    sgb = sb.ph("sgb", [128, W], BF16)
    mbf = sb.ph("mbf", [128, 8 * W], BF16)
    mlpb = [sb.ph("mlpb", [128, H], F32) for _ in range(2)]
    acc_guard = {}

    p.phase(23)
    fence()
    for ffc in range(4):
        p.wait("gpsimd", st, st.n)
        p.dma("gpsimd", wgc[:].rearrange("p (k m) -> p k m", k=8),
              d["w_gateT"][:, :, 1024 * ffc:1024 * (ffc + 1)]
              .rearrange("k p m -> p k m"), sG)
        p.dma("gpsimd", wuc[:].rearrange("p (k m) -> p k m", k=8),
              d["w_upT"][:, :, 1024 * ffc:1024 * (ffc + 1)]
              .rearrange("k p m -> p k m"), sG)
        p.dma("gpsimd", wdc[:].rearrange("p (k m) -> p k m", k=8),
              d["w_downT"][8 * ffc:8 * (ffc + 1)].rearrange("k p m -> p k m"), sG)
        wch_done = sG.n
        for b in range(Bb):
            p.wait("sync", st, st.n)
            p.dmat("sync", h2fm[:].rearrange("p (k t) -> p k t", k=8),
                   h2sp.ap()[b].rearrange("c q o -> (c q) o"), sL)
            h2_ld = sL.n
            p.wait("vector", sL, h2_ld)
            for k in range(8):
                sl = h2fm[:, k * W:(k + 1) * W]
                p.op("vector", lambda e, s=sl, k=k: e.tensor_scalar_mul(
                    s, s, gam2_s[:, k:k + 1]), sv)
            fold_done = sv.n
            p.wait("tensor", sv, fold_done)
            if b == 0:
                p.wait("tensor", sG, wch_done)
            for fft in range(8):
                bank_wait("tensor", [0, 2])
                for k in range(8):
                    p.op("tensor", lambda e, k=k, fft=fft: e.matmul(
                        PS[0][:, 0:512], wgc[:, k * 1024 + 128 * fft:k * 1024 + 128 * fft + 128],
                        h2fm[:, W * k:W * (k + 1)], start=(k == 0), stop=(k == 7)),
                        st if k == 7 else None)
                for k in range(8):
                    p.op("tensor", lambda e, k=k, fft=fft: e.matmul(
                        PS[2][:, 0:512], wuc[:, k * 1024 + 128 * fft:k * 1024 + 128 * fft + 128],
                        h2fm[:, W * k:W * (k + 1)], start=(k == 0), stop=(k == 7)),
                        st if k == 7 else None)
                p.wait("scalar", st, st.n)
                p.op("scalar", lambda e: e.activation(sgb[:], PS[0][:, 0:512], AF.Silu), ss)
                mark_bank(0, ss)
                p.wait("vector", ss, ss.n)
                TT("vector", mbf[:, W * fft:W * (fft + 1)], sgb[:], PS[2][:, 0:512],
                   ALU.mult, sv)
                mark_bank(2, sv)
            m_done = sv.n
            p.wait("tensor", sv, m_done)
            for qt in range(4):
                bank_wait("tensor", [3, 4])
                for fft in range(8):
                    for half in range(2):
                        p.op("tensor", lambda e, qt=qt, fft=fft, half=half: e.matmul(
                            PS[3 + half][:, 0:512],
                            mbf[:, W * fft + 128 * qt:W * fft + 128 * (qt + 1)],
                            wdc[:, fft * 1024 + 512 * half:fft * 1024 + 512 * (half + 1)],
                            start=(fft == 0), stop=(fft == 7)),
                            st if (fft == 7 and half == 1) else None)
                mb = mlpb[qt % 2]
                g = acc_guard.get(("buf", qt % 2))
                p.wait("vector", st, st.n)
                if g is not None:
                    p.wait("vector", sG, g)
                for half in range(2):
                    p.op("vector", lambda e, mb=mb, half=half: e.tensor_copy(
                        mb[:, 512 * half:512 * (half + 1)], PS[3 + half][:, 0:512]), sv)
                mark_bank(3, sv)
                mark_bank(4, sv)
                prev = acc_guard.get((b, qt))
                p.wait("gpsimd", sv, sv.n)
                if prev is not None:
                    p.wait("gpsimd", sG, prev)
                p.dma("gpsimd", mlpacc.ap()[b, qt], mb[:], sG,
                      **({} if ffc == 0 else {"accum_op": ALU.add}))
                acc_guard[(b, qt)] = sG.n
                acc_guard[("buf", qt % 2)] = sG.n

    sb.open_phase()
    xt3 = sb.ph("xt3", [128, H], F32)
    hd3 = sb.ph("hd3", [128, H], F32)
    mp3 = sb.ph("mp3", [128, H], F32)
    hb3 = sb.ph("hb3", [128, H], BF16)
    o3 = sb.ph("o3", [128, H], F32)

    p.phase(1)
    fence()
    for b in range(Bb):
        for c in range(4):
            p.wait("sync", sL, sL.n)
            p.wait("sync", sv, sv.n)
            prev = acc_guard.get((b, c))
            if prev is not None:
                p.wait("sync", sG, prev)
            p.dma("sync", xt3[:], d["xin"][b, 3 + 128 * c:131 + 128 * c, :], sL)
            p.dma("sync", hd3[:], hidsp.ap()[b, c], sL)
            p.dma("sync", mp3[:], mlpacc.ap()[b, c], sL)
            p.dma("sync", hb3[:], hsp.ap()[b, c], sL)
            f_ld = sL.n
            p.wait("vector", sL, f_ld)
            TT("vector", o3[:], xt3[:], hd3[:], ALU.add, sv)
            TT("vector", o3[:], o3[:], mp3[:], ALU.add, sv)
            TT("vector", o3[:], o3[:], hb3[:], ALU.add, sv)
            p.wait("sync", sv, sv.n)
            p.dma("sync", out_d[b, 128 * c:128 * (c + 1), :], o3[:], sL)
    p.wait("sync", sL, sL.n)
    p.wait("sync", sS, sS.n)
    p.wait("sync", sG, sG.n)

    with nc.Block() as blk:
        for s in p.sems:
            s.h = nc.alloc_semaphore(s.name)

        def runner(engine_name):
            def run(eng):
                for fn in p.ops[engine_name]:
                    fn(eng)
            return run
        blk.sync(runner("sync"))
        blk.scalar(runner("scalar"))
        blk.vector(runner("vector"))
        blk.tensor(runner("tensor"))
        blk.gpsimd(runner("gpsimd"))
    return nc


_NC_CACHE = []


def kernel(**inputs):
    in_maps = _stage_inputs(inputs)
    if not _NC_CACHE:
        _NC_CACHE.append(build())
    nc = _NC_CACHE[0]
    res = run_bass_kernel_spmd(nc, in_maps, core_ids=list(range(NCORE)))
    return np.concatenate([np.asarray(r["out"], dtype=np.float32)
                           for r in res.results], axis=1)
